# version 38
# baseline (speedup 1.0000x reference)
import numpy as np
import concourse.bass as bass
import concourse.mybir as mybir
from concourse.bass_utils import run_bass_kernel_spmd
from contextlib import ExitStack

F32 = mybir.dt.float32
BF16 = mybir.dt.bfloat16
I32 = mybir.dt.int32
AX = mybir.AxisListType
ALU = mybir.AluOpType
ACT = mybir.ActivationFunctionType

FULL = dict(T=2048, D=1024, NF=4096)
N_CORES = 8


class Ev:
    __slots__ = ("eng", "fn", "waits", "inc")
    def __init__(self, eng, fn, waits, inc):
        self.eng, self.fn, self.waits, self.inc = eng, fn, waits, inc


def build_program(T, D, NF, debug=False):
    TC = T // 128
    DC = D // 128
    NC = NF // 128
    JW = D // 512
    AG = DC // 4
    HG = DC // 4
    NQ = TC // 4
    CH = 512 if T % 512 == 0 else T
    NCH = T // CH
    SUB = CH // 128
    NB = JW * SUB
    R1 = 3 if NB >= 3 else NB
    NSLOT = 8
    NPAIR = 4
    assert NB <= 8 and DC <= 8 and TC % 4 == 0 and NC % 2 == 0 and NC <= 32
    EPS = 1e-5

    nc = bass.Bass("TRN2", target_bir_lowering=False)

    x_h = nc.dram_tensor("x", [T, D], F32, kind="ExternalInput")
    wq_h = nc.dram_tensor("wq", [D, D], F32, kind="ExternalInput")
    w1_h = nc.dram_tensor("w1", [D, NF], F32, kind="ExternalInput")
    b1_h = nc.dram_tensor("b1", [NF], F32, kind="ExternalInput")
    w2_h = nc.dram_tensor("w2", [NF, D], F32, kind="ExternalInput")
    b2_h = nc.dram_tensor("b2", [D], F32, kind="ExternalInput")
    o_h = nc.dram_tensor("out", [T, D], F32, kind="ExternalOutput")
    w2scr = nc.dram_tensor("w2scr", [NF, D], BF16, kind="Internal")
    if debug:
        dbg_xt = nc.dram_tensor("dbg_xt", [128, TC, D], BF16, kind="ExternalOutput")
        dbg_xf = nc.dram_tensor("dbg_xf", [128, DC, T], BF16, kind="ExternalOutput")
        dbg_gs = nc.dram_tensor("dbg_gs", [128, DC, D], BF16, kind="ExternalOutput")
        dbg_mt = nc.dram_tensor("dbg_mt", [128, DC, D], BF16, kind="ExternalOutput")
        dbg_sc = nc.dram_tensor("dbg_sc", [128, SUB, D], F32, kind="ExternalOutput")
        dbg_h1 = nc.dram_tensor("dbg_h1", [128, NC, CH], BF16, kind="ExternalOutput")

    ARENA = 212700
    arena_ctx = ExitStack()
    arena = arena_ctx.enter_context(nc.sbuf_tensor("arena", [128, ARENA], mybir.dt.uint8))
    arena_base = nc.lookup_mloc(arena).addr

    off = [0]
    def sb(name, shape, dtype):
        import functools, operator
        per = functools.reduce(operator.mul, shape[1:]) * mybir.dt.size(dtype)
        h = nc.alloc_sbuf_tensor_at(name, shape, dtype, offset=arena_base + off[0])
        off[0] += (per + 31) // 32 * 32
        return h

    W1 = sb("W1s", [128, DC, NF], BF16)
    XF = sb("XFs", [128, DC, T], BF16)
    MT = sb("MTs", [128, DC, D], BF16)
    XBUF = sb("XBUFs", [128, max(SUB, 4), D], F32)
    B1 = sb("B1s", [128, NC], F32)
    B2R = sb("B2Rs", [128, D], BF16)
    STAT = sb("STATs", [128, TC * 18], F32)
    DUM = sb("DUMs", [128, 5], F32)
    tscr_off = off[0]
    TSCR = sb("TSCRs", [128, 3, 512], F32)
    FSCR = sb("FSCRs", [128, 3, 512], F32)
    save_off = off[0]
    off[0] = tscr_off
    B1STG = sb("B1STGs", [128, 128], F32)
    IOT = sb("IOTs", [128, 256], I32)
    IDTF = sb("IDTFs", [128, 32], F32)
    IDT = sb("IDTs", [128, 128], BF16)
    off[0] = save_off
    ab_base = off[0]
    XT = sb("XTs", [128, TC, D], BF16)
    WQR = sb("WQRs", [128, 4, D], BF16)
    gs_off = off[0]
    GS = sb("GSs", [128, DC, D], BF16)
    XGS = nc.alloc_sbuf_tensor_at("XGSs", [128, 2, D], F32, offset=arena_base + gs_off)
    off[0] = ab_base
    H1 = sb("H1s", [128, NC, CH], BF16)
    W2S = sb("W2Ss", [128, NSLOT, D], BF16)
    SC = sb("SCs", [128, SUB, D], F32)
    end_b = off[0]
    reg_a_end = ab_base + (XT.shape[1] * XT.shape[2] * 2 + 31) // 32 * 32 \
        + (WQR.shape[1] * WQR.shape[2] * 2 + 31) // 32 * 32 \
        + (GS.shape[1] * GS.shape[2] * 2 + 31) // 32 * 32
    off[0] = reg_a_end
    XLDT = sb("XLDTs", [128, 2, D], F32)
    off[0] = reg_a_end
    WQR2 = sb("WQR2s", [128, 4, D], BF16)
    off[0] = max(end_b, reg_a_end + 2 * D * 4)
    assert off[0] <= ARENA, f"sbuf overflow {off[0]}"

    ctx = ExitStack()
    PB = [ctx.enter_context(nc.psum_tensor(f"pb{i}", [128, 512], F32)) for i in range(8)]

    sem_names = [
        "s_x0", "s_x1", "s_x2", "s_x3", "s_x4", "s_x5", "s_x6", "s_x7", "s_x8", "s_x9", "s_xc", "s_bc", "s_bcg", "s_w1",
        "s_wq0", "s_wq1", "s_wq2", "s_wq3", "s_wq4", "s_wq5", "s_wq6", "s_wq7", "s_wqc", "s_w2s",
        "s_w1b0", "s_w1b1", "s_w1b2", "s_w1b3", "s_w1b4", "s_w1b5", "s_w1b6", "s_w1b7",
        "s_w2c", "s_iotf", "s_b1t", "s_b1ev",
        "s_v1", "s_vc", "s_g1", "s_gxa", "s_gxd",
        "s_a1", "s_a2", "s_xn",
        "s_tq", "s_xfA", "s_xfB",
        "s_g", "s_gev", "s_m", "s_mev", "s_mevA",
        "s_s", "s_sev", "s_h1", "s_h1ev", "s_f",
        "s_epA", "s_epB", "s_epC", "s_epC2", "s_epD", "s_epF",
        "s_out", "s_dm", "s_dbg",
    ] + [f"s_w2rp{i}" for i in range(NPAIR)]
    S = {n: ctx.enter_context(nc.semaphore(n)) for n in sem_names}
    C = {n: 0 for n in sem_names}

    evs = []
    def emit(eng, fn, waits=(), inc=None, amt=1):
        w = [(S[s], v) for (s, v) in waits if v > 0]
        if inc is not None:
            C[inc] += amt
            inc_t = (S[inc], amt)
        else:
            inc_t = None
        evs.append(Ev(eng, fn, w, inc_t))


    emit("gpsimd", lambda e: e.memset(DUM[:, :], 0.0), inc="s_dm")
    emit("gpsimd", lambda e: e.memset(DUM[:, 4:5], EPS), waits=[("s_dm", 1)], inc="s_dm")
    emit("gpsimd", lambda e: e.iota(IOT[:, 0:128], [[1, 128]], channel_multiplier=0),
         inc="s_g1")
    emit("gpsimd", lambda e: e.iota(IOT[:, 128:256], [[0, 128]], channel_multiplier=1),
         inc="s_g1")
    emit("vector", lambda e: e.tensor_tensor(IDT[:, :], IOT[:, 0:128], IOT[:, 128:256],
                                             ALU.is_equal),
         waits=[("s_g1", 2)], inc="s_iotf")
    emit("vector", lambda e: e.tensor_copy(IDTF[:, 0:32], IDT[:, 0:32]),
         waits=[("s_iotf", 1)], inc="s_iotf")

    NWQ = HG * DC
    NXS = 8
    xldt_tiles = [i for i in range(TC) if i % NXS in (4, 5)]
    assert D == 1024 or TC <= 6 or True
    XLDT_LAST = (max(xldt_tiles) + 1) if xldt_tiles else 0

    def wq_slot_ap(L):
        s = L % 8
        return WQR[:, s, :] if s < 4 else WQR2[:, s - 4, :]

    def wq_load(L):
        r = L % DC
        wts = []
        if 4 <= L < 8 and XLDT_LAST:
            wts = [("s_gxa", 2 * XLDT_LAST), ("s_v1", 2 * XLDT_LAST)]
        emit("gpsimd", lambda e, L=L, r=r: e.dma_start(
            out=wq_slot_ap(L), in_=wq_h.ap()[r * 128:(r + 1) * 128, :]),
            waits=wts, inc=f"s_wq{L % 8}", amt=16)

    emit("gpsimd", lambda e: e.dma_start(
        out=B2R[:, :], in_=bass.AP(b2_h, 0, [[0, 128], [1, D]])), inc="s_bcg", amt=16)
    for L in range(min(8, DC)):
        wq_load(L)
    NBLK = NF // 512
    for nblk in range(NBLK):
        for hc in range(DC):
            emit("gpsimd", lambda e, hc=hc, nblk=nblk: e.dma_start(
                out=W1[:, hc, nblk * 512:(nblk + 1) * 512],
                in_=w1_h.ap()[hc * 128:(hc + 1) * 128, nblk * 512:(nblk + 1) * 512]),
                waits=[("s_v1", 2 * TC)], inc=f"s_w1b{nblk}", amt=16)
    emit("gpsimd", lambda e: e.dma_start(out=w2scr.ap(), in_=w2_h.ap()),
         waits=[("s_v1", 2 * TC)], inc="s_w2s", amt=16)


    emit("sync", lambda e: e.dma_start(
        out=B1STG[0:NC, 0:128], in_=b1_h.ap().rearrange("(a q) -> a q", q=128)),
        inc="s_bc", amt=16)
    emit("tensor", lambda e: e.transpose(PB[0][:, 0:NC], B1STG[0:NC, 0:128],
                                         IDTF[0:NC, 0:NC]),
         waits=[("s_bc", 16), ("s_iotf", 2)], inc="s_b1t")
    emit("vector", lambda e: e.tensor_copy(B1[:, :], PB[0][:, 0:NC]),
         waits=[("s_b1t", 1)], inc="s_b1ev")

    emit("scalar", lambda e: e.activation(DUM[:, 1:2], DUM[:, 0:1], ACT.Sqrt),
         waits=[("s_dm", 1)])


    def _xsl(i, a, b):
        s = i % NXS
        if s < 4:
            return XBUF[:, s, a:b]
        if s < 6:
            return XLDT[:, s - 4, a:b]
        if s == 6:
            return bass.AP(TSCR, 512 + a, [[3 * 512, 128], [1, b - a]])
        if s == 7:
            return bass.AP(FSCR, a, [[3 * 512, 128], [1, b - a]])
        return XGS[:, s - 8, a:b]

    def xslot(i):
        return _xsl(i, 0, D)

    def p1_load(i):
        slot = i % NXS
        w = []
        if i >= NXS:
            w = [("s_gxa", 2 * (i - NXS + 1)), ("s_v1", 2 * (i - NXS + 1))]
        emit("sync", lambda e, i=i: e.dma_start(
            out=xslot(i), in_=x_h.ap()[i * 128:(i + 1) * 128, :]),
            waits=w, inc=f"s_x{slot}", amt=16)

    def p1_head_lo(i):
        slot = i % NXS
        c0 = i * 18
        sx = f"s_x{slot}"
        D2 = D // 2
        emit("vector", lambda e, i=i, c0=c0, D2=D2: e.bn_stats(
            STAT[:, c0:c0 + 6], _xsl(i, 0, D2)),
            waits=[(sx, 16 * (i // NXS + 1))], inc="s_v1")

    def p1_head_hi(i):
        c0 = i * 18
        D2 = D // 2
        emit("vector", lambda e, i=i, c0=c0, D2=D2: e.bn_stats(
            STAT[:, c0 + 6:c0 + 12], _xsl(i, D2, D)), inc="s_v1")

    vc_at = {}

    def p1_stage1(i):
        c0 = i * 18
        emit("vector", lambda e, c0=c0: e.bn_aggr(
            STAT[:, c0 + 12:c0 + 14], STAT[:, c0:c0 + 12]),
            waits=[("s_v1", 2 * (i + 1))], inc="s_vc")
        vc_at[("aggr", i)] = C["s_vc"]

    def p1_stage2(i):
        c0 = i * 18
        emit("scalar", lambda e, c0=c0: e.activation(
            STAT[:, c0 + 14:c0 + 15], STAT[:, c0 + 13:c0 + 14], ACT.Sqrt,
            bias=DUM[:, 4:5], scale=1.0),
            waits=[("s_vc", vc_at[("aggr", i)])] + ([("s_dm", 2)] if i == 0 else []),
            inc="s_a2")
        emit("vector", lambda e, c0=c0: e.reciprocal(
            STAT[:, c0 + 15:c0 + 16], STAT[:, c0 + 14:c0 + 15]),
            waits=[("s_a2", i + 1)], inc="s_vc")
        vc_at[("r", i)] = C["s_vc"]
        emit("vector", lambda e, c0=c0: e.tensor_scalar(
            STAT[:, c0 + 16:c0 + 17], STAT[:, c0 + 12:c0 + 13], STAT[:, c0 + 15:c0 + 16],
            -1.0, op0=ALU.mult, op1=ALU.mult),
            waits=[("s_vc", vc_at[("r", i)])], inc="s_vc")
        vc_at[("bias2", i)] = C["s_vc"]

    def p1_stage3(i):
        c0 = i * 18
        D2 = D // 2
        emit("scalar", lambda e, i=i, c0=c0, D2=D2: e.activation(
            XT[:, i, 0:D2], _xsl(i, 0, D2), ACT.Identity,
            bias=STAT[:, c0 + 16:c0 + 17], scale=STAT[:, c0 + 15:c0 + 16]),
            waits=[("s_vc", vc_at[("bias2", i)])], inc="s_gxa")
        emit("scalar", lambda e, i=i, c0=c0, D2=D2: e.activation(
            XT[:, i, D2:D], _xsl(i, D2, D), ACT.Identity,
            bias=STAT[:, c0 + 16:c0 + 17], scale=STAT[:, c0 + 15:c0 + 16]),
            inc="s_gxa")

    NPH = (DC + 3) // 4
    xfA_cum, xfB_cum = [], []

    def g_round(w, b, ag, r):
        for ai in range(4):
            wts = []
            if r == 0 and ai == 0:
                if w == 0:
                    wts.append(("s_b1ev", 1))
                else:
                    wts.append(("s_gev", 4 * w))
            if w == 0 and ai == 0:
                wts.append(("s_gxa", 2 * (r + 1)))
            emit("tensor", lambda e, r=r, a=ai, b=b, ag=ag: e.matmul(
                PB[a][:, :], XT[:, r, (ag * 4 + a) * 128:(ag * 4 + a + 1) * 128],
                XT[:, r, b * 512:(b + 1) * 512],
                start=(r == 0), stop=(r == TC - 1)),
                waits=wts, inc=("s_g" if r == TC - 1 else None))

    def g_evacs(w, b, ag):
        for ai in range(4):
            emit("scalar", lambda e, a=ai, b=b, ag=ag: e.copy(
                GS[:, ag * 4 + a, b * 512:(b + 1) * 512], PB[a][:, :]),
                waits=[("s_g", 4 * w + ai + 1)], inc="s_gev")

    def tp_quad(q):
        for P in range(NPH):
            for j in range(4):
                for hcl in range(4):
                    hc = P * 4 + hcl
                    if hc >= DC:
                        continue
                    bank = 4 + hcl
                    wts = []
                    if j == 0 and hcl == 0:
                        wts += [("s_gxa", 2 * (4 * q + 4)), ("s_iotf", 2)]
                        if P == 0 and q > 0:
                            wts += [("s_xfA", xfA_cum[q - 1]), ("s_xfB", xfB_cum[q - 1])]
                    if j == 0 and P > 0:
                        if hcl < 2:
                            wts.append(("s_xfA", C["s_xfA"] - 1 + hcl))
                        else:
                            wts.append(("s_xfB", C["s_xfB"] - 1 + (hcl - 2)))
                    emit("tensor", lambda e, q=q, j=j, hc=hc, bank=bank: e.matmul(
                        PB[bank][:, j * 128:(j + 1) * 128],
                        XT[:, 4 * q + j, hc * 128:(hc + 1) * 128], IDT[:, :],
                        start=True, stop=True),
                        waits=wts, inc=("s_tq" if j == 3 else None))
            for hcl in range(4):
                hc = P * 4 + hcl
                if hc >= DC:
                    continue
                eng = "scalar" if hcl < 2 else "vector"
                sem = "s_xfA" if hcl < 2 else "s_xfB"
                emit(eng, lambda e, q=q, hc=hc, hcl=hcl:
                     (e.copy if hcl < 2 else e.tensor_copy)(
                         XF[:, hc, q * 512:(q + 1) * 512], PB[4 + hcl][:, :]),
                     waits=[("s_tq", C["s_tq"] - (3 - hcl))], inc=sem)
        xfA_cum.append(C["s_xfA"])
        xfB_cum.append(C["s_xfB"])

    for i in range(TC + 2):
        if i < TC:
            p1_load(i)
        if 1 <= i <= TC:
            p1_stage1(i - 1)
        if i < TC:
            p1_head_lo(i)
        if 1 <= i <= TC:
            p1_stage2(i - 1)
        if i < TC:
            p1_head_hi(i)
        if 2 <= i <= TC + 1:
            p1_stage3(i - 2)
            g_round(0, 0, 0, i - 2)
            if (i - 2) % 4 == 3:
                tp_quad((i - 2) // 4)
    g_evacs(0, 0, 0)
    XF_A_ALL = C["s_xfA"]
    XF_B_ALL = C["s_xfB"]

    w = 1
    for b in range(JW):
        for ag in range(AG):
            if b == 0 and ag == 0:
                continue
            if b == 1 and ag == 0 and JW == 2 and AG == 2:
                for j in range(4):
                    for i2 in range(4):
                        wts = []
                        if j == 0 and i2 == 0:
                            wts = [("s_gev", 8), ("s_iotf", 2)]
                        emit("tensor", lambda e, j=j, i2=i2: e.matmul(
                            PB[j][:, i2 * 128:(i2 + 1) * 128],
                            GS[:, 4 + i2, j * 128:(j + 1) * 128], IDT[:, :],
                            start=True, stop=True),
                            waits=wts, inc=("s_g" if i2 == 3 else None))
                for j in range(4):
                    emit("scalar", lambda e, j=j: e.copy(
                        GS[:, j, 512:1024], PB[j][:, :]),
                        waits=[("s_g", 8 + j + 1)], inc="s_gev")
                w += 1
                continue
            for r in range(TC):
                g_round(w, b, ag, r)
            g_evacs(w, b, ag)
            w += 1
    G_EV_ALL = C["s_gev"]

    def emit_M():
        for hg in range(HG):
            for r in range(DC):
                L = hg * DC + r
                nmm = 4 * JW
                for mi, (hci, jw) in enumerate([(a2, b2) for a2 in range(4) for b2 in range(JW)]):
                    bank = hci * JW + jw
                    wts = []
                    if mi == 0:
                        wts.append((f"s_wq{r % 8}", 16))
                        wts.append(("s_gev", (JW - 1) * DC + r + 1))
                        if r == 0 and hg > 0:
                            wts.append(("s_mev", 2 * JW * hg))
                            wts.append(("s_mevA", 2 * JW * hg))
                        if r == 0 and hg == 0:
                            wts.append(("s_gev", G_EV_ALL))
                            if FFN1_FIRST:
                                wts.append(("s_h1ev", NC))
                            else:
                                wts.append(("s_xfA", XF_A_ALL))
                                wts.append(("s_xfB", XF_B_ALL))
                    if r == DC - 1:
                        inc = "s_m"
                    elif mi == nmm - 1:
                        inc = "s_wqc"
                    else:
                        inc = None
                    emit("tensor", lambda e, hg=hg, r=r, hci=hci, jw=jw, L=L, bank=bank: e.matmul(
                        PB[bank][:, :], wq_slot_ap(r)[:, (hg * 4 + hci) * 128:(hg * 4 + hci + 1) * 128],
                        GS[:, r, jw * 512:(jw + 1) * 512],
                        start=(r == 0), stop=(r == DC - 1)),
                        waits=wts, inc=inc)
            for hci in range(4):
                for jw in range(JW):
                    idx = hci * JW + jw
                    if idx % 2 == 0:
                        emit("vector", lambda e, hg=hg, hci=hci, jw=jw, idx=idx: e.tensor_copy(
                            MT[:, hg * 4 + hci, jw * 512:(jw + 1) * 512], PB[idx][:, :]),
                            waits=[("s_m", 4 * JW * hg + idx + 1)], inc="s_mev")
                    else:
                        emit("scalar", lambda e, hg=hg, hci=hci, jw=jw, idx=idx: e.copy(
                            MT[:, hg * 4 + hci, jw * 512:(jw + 1) * 512], PB[idx][:, :]),
                            waits=[("s_m", 4 * JW * hg + idx + 1)], inc="s_mevA")
        return C["s_m"], C["s_mev"] + C["s_mevA"]

    M_ALL = 4 * JW * HG
    M_EV_ALL = 4 * JW * HG
    FFN1_FIRST = NC * CH * 2 <= TC * D * 2
    if not FFN1_FIRST:
        m_all, m_ev_all = emit_M()
        assert m_all == M_ALL and m_ev_all == M_EV_ALL

    emit("sync", lambda e: e.dma_start(
        out=XBUF[:, 0:SUB, :], in_=x_h.ap()[0:CH, :].rearrange("(s p) e -> p s e", p=128)),
        waits=[("s_gxa", 2 * TC), ("s_v1", 2 * TC)], inc="s_xc", amt=16)

    emit("scalar", lambda e: e.activation(DUM[:, 2:3], DUM[:, 0:1], ACT.Gelu))

    if debug:
        dw = [("s_mev", M_EV_ALL // 2), ("s_mevA", M_EV_ALL // 2),
              ("s_gev", G_EV_ALL), ("s_gxa", 2 * TC),
              ("s_xfA", XF_A_ALL), ("s_xfB", XF_B_ALL)]
        emit("sync", lambda e: e.dma_start(out=dbg_xt.ap(), in_=XT[:, :, :]),
             waits=dw, inc="s_dbg", amt=16)
        emit("sync", lambda e: e.dma_start(out=dbg_xf.ap(), in_=XF[:, :, :]), inc="s_dbg", amt=16)
        emit("sync", lambda e: e.dma_start(out=dbg_gs.ap(), in_=GS[:, :, :]), inc="s_dbg", amt=16)
        emit("sync", lambda e: e.dma_start(out=dbg_mt.ap(), in_=MT[:, :, :]), inc="s_dbg", amt=16)

    w2rp_cnt = [0] * NPAIR
    w2_avail = {}
    pending_tail = [None]
    for c in range(NCH):
        npass = 2 if (c == NCH - 1 and JW == 2) else 1

        def emit_refills(ps, c=c):
            if ps == 0:
                for pk in range(NC // 2):
                    nk0 = 2 * pk
                    Rg = NC * c + nk0
                    if Rg < NSLOT:
                        wts = [("s_m", M_ALL), ("s_w2s", 16)]
                    else:
                        Rp = Rg - NSLOT + 1
                        c_p, nk_p = Rp // NC, Rp % NC
                        if nk_p < NC - 1:
                            wts = [("s_w2c", c_p * (NC - 1) + nk_p + 1)]
                        else:
                            wts = [("s_f", NB * c_p + (SUB if (c_p == NCH - 1 and JW == 2) else NB))]
                    p = (nk0 % NSLOT) // 2
                    w2rp_cnt[p] += 16
                    w2_avail[(c, 0, nk0)] = w2_avail[(c, 0, nk0 + 1)] = (p, w2rp_cnt[p])
                    emit("sync", lambda e, nk0=nk0: e.dma_start(
                        out=W2S[:, nk0 % NSLOT:nk0 % NSLOT + 2, :],
                        in_=w2scr.ap()[nk0 * 128:(nk0 + 2) * 128, :].rearrange(
                            "(k p) e -> p k e", p=128)),
                        waits=wts, inc=f"s_w2rp{p}", amt=16)
            else:
                base = NCH * (NC - 1)
                for r in range(NC - NSLOT - 2, -1, -2):
                    wts = [("s_w2c", base + NC - (r + NSLOT))]
                    p = (r % NSLOT) // 2
                    w2rp_cnt[p] += 16
                    w2_avail[(c, 1, r)] = w2_avail[(c, 1, r + 1)] = (p, w2rp_cnt[p])
                    emit("sync", lambda e, r=r: e.dma_start(
                        out=W2S[:, r % NSLOT:r % NSLOT + 2, :],
                        in_=w2scr.ap()[r * 128:(r + 2) * 128, :].rearrange(
                            "(k p) e -> p k e", p=128)),
                        waits=wts, inc=f"s_w2rp{p}", amt=16)

        emit_refills(0)

        bank0 = 4 if (c == 0 and FFN1_FIRST) else 0
        for na in range(NC):
            bank = bank0 + na % R1
            wts = []
            if na == 0:
                wts += [("s_xfA", xfA_cum[min(c, NQ - 1)]),
                        ("s_xfB", xfB_cum[min(c, NQ - 1)])]
            if c == 0:
                if na == 0:
                    wts += [("s_xfA", XF_A_ALL), ("s_xfB", XF_B_ALL)]
                    if not FFN1_FIRST:
                        wts += [("s_mev", M_EV_ALL // 2), ("s_mevA", M_EV_ALL // 2)]
                elif na >= R1:
                    wts = [("s_h1ev", NC * c + (na - R1) + 1)]
            else:
                if na == 0:
                    wts += [(f"s_w1b{b}", 16 * DC) for b in range(NF // 512)]
                if na < R1:
                    wts += [("s_epF", min(3, NB) * (c - 1) + (na % R1) + 1)]
                else:
                    wts = [("s_h1ev", NC * c + (na - R1) + 1)]
            for hc in range(DC):
                hw = list(wts) if hc == 0 else []
                if c == 0 and hc == 0:
                    hw.append((f"s_w1b{na // 4}", 16 * DC))
                emit("tensor", lambda e, na=na, hc=hc, c=c, bank=bank: e.matmul(
                    PB[bank][:, 0:CH], W1[:, hc, na * 128:(na + 1) * 128],
                    XF[:, hc, c * CH:(c + 1) * CH],
                    start=(hc == 0), stop=(hc == DC - 1)),
                    waits=hw, inc=("s_h1" if hc == DC - 1 else None))
        for na in range(NC):
            emit("scalar", lambda e, na=na, bank=bank0 + na % R1: e.activation(
                H1[:, na, :], PB[bank][:, 0:CH], ACT.Gelu, bias=B1[:, na:na + 1], scale=1.0),
                waits=[("s_h1", NC * c + na + 1), ("s_b1ev", 1)], inc="s_h1ev")

        if c == 0 and FFN1_FIRST:
            m_all, m_ev_all = emit_M()
            assert m_all == M_ALL and m_ev_all == M_EV_ALL

        if pending_tail[0] is not None:
            pending_tail[0]()
            pending_tail[0] = None

        if npass == 2:
            emit_refills(1)

        for g in range(NB):
            w2, ts = g // SUB, g % SUB
            wts = []
            if c == 0:
                if g == 0:
                    wts.append(("s_mev", M_EV_ALL // 2))
                    wts.append(("s_mevA", M_EV_ALL // 2))
                if not FFN1_FIRST and g < R1:
                    last_na = max(n for n in range(NC) if n % R1 == g)
                    wts.append(("s_h1ev", NC * c + last_na + 1))
            elif g < R1:
                last_na = max(n for n in range(NC) if n % R1 == g)
                wts.append(("s_h1ev", NC * c + last_na + 1))
            else:
                wts.append(("s_epD", NB * (c - 1) + g + 1))
            for hc in range(DC):
                emit("tensor", lambda e, g=g, w2=w2, ts=ts, hc=hc, c=c: e.matmul(
                    PB[g][:, :], XF[:, hc, c * CH + ts * 128:c * CH + (ts + 1) * 128],
                    MT[:, hc, w2 * 512:(w2 + 1) * 512],
                    start=(hc == 0), stop=(hc == DC - 1)),
                    waits=(wts if hc == 0 else []), inc=("s_s" if hc == DC - 1 else None))
        for g in range(NB):
            w2, ts = g // SUB, g % SUB
            emit("vector", lambda e, g=g, w2=w2, ts=ts: e.tensor_copy(
                SC[:, ts, w2 * 512:(w2 + 1) * 512], PB[g][:, :]),
                waits=[("s_s", NB * c + g + 1), ("s_epC", NB * c)], inc="s_sev")

        split2 = (c == NCH - 1 and JW == 2)
        if not split2:
            for nk in range(NC):
                wts = []
                if nk == 0:
                    wts = [("s_h1ev", NC * (c + 1)), ("s_sev", NB * (c + 1))]
                else:
                    wts = [("s_h1ev", NC * c + nk + 1)]
                pi, cnt = w2_avail[(c, 0, nk)]
                wts.append((f"s_w2rp{pi}", cnt))
                pairs = [(ts, ejw) for ts in range(SUB) for ejw in range(JW)]
                for mi, (ts, ejw) in enumerate(pairs):
                    bank = ts * JW + ejw
                    if nk == NC - 1:
                        inc = "s_f"
                    elif mi == len(pairs) - 1:
                        inc = "s_w2c"
                    else:
                        inc = None
                    emit("tensor", lambda e, nk=nk, ts=ts, ejw=ejw, bank=bank: e.matmul(
                        PB[bank][:, :], H1[:, nk, ts * 128:(ts + 1) * 128],
                        W2S[:, nk % NSLOT, ejw * 512:(ejw + 1) * 512],
                        start=(nk == 0), stop=(nk == NC - 1)),
                        waits=(wts if mi == 0 else []), inc=inc)
        else:
            for ep in range(2):
                order = list(range(NC)) if ep == 0 else list(range(NC - 1, -1, -1))
                for vi, nk in enumerate(order):
                    wts = []
                    if ep == 0 and nk == 0:
                        wts = [("s_h1ev", NC * (c + 1)), ("s_sev", NB * (c + 1))]
                    elif ep == 0:
                        wts = [("s_h1ev", NC * c + nk + 1)]
                    if (c, ep, nk) in w2_avail:
                        pi, cnt = w2_avail[(c, ep, nk)]
                        wts.append((f"s_w2rp{pi}", cnt))
                    first, last = (vi == 0), (vi == NC - 1)
                    for ts in range(SUB):
                        bank = 2 * ts + ep
                        if last:
                            inc = "s_f"
                        elif ts == SUB - 1:
                            inc = "s_w2c"
                        else:
                            inc = None
                        emit("tensor", lambda e, nk=nk, ts=ts, ep=ep, bank=bank,
                             first=first, last=last: e.matmul(
                            PB[bank][:, :], H1[:, nk, ts * 128:(ts + 1) * 128],
                            W2S[:, nk % NSLOT, ep * 512:(ep + 1) * 512],
                            start=first, stop=last),
                            waits=(wts if ts == 0 else []), inc=inc)

        if debug and c == 0:
            emit("sync", lambda e: e.dma_start(out=dbg_sc.ap(), in_=SC[:, :, :]),
                 waits=[("s_f", NB)], inc="s_dbg", amt=16)
            emit("sync", lambda e: e.dma_start(out=dbg_h1.ap(), in_=H1[:, :, :]), inc="s_dbg", amt=16)

        last_chunk = (c == NCH - 1)
        if last_chunk:
            emit("scalar", lambda e: e.activation(DUM[:, 3:4], DUM[:, 0:1], ACT.Silu))
        NFA = min(3, NB)
        if last_chunk and JW == 2:
            seq = [2 * t for t in range(SUB)] + [2 * t + 1 for t in range(SUB)]
        else:
            seq = list(range(NB))
        def buf(idx, _seq=seq, _NFA=NFA):
            return FSCR[:, idx, :] if idx < _NFA else PB[_seq[idx]][:, :]
        def ep_A(idx, c=c):
            bank = seq[idx]
            ejw = bank % JW
            b = buf(idx)
            emit("vector", lambda e, b=b, bank=bank, ejw=ejw: e.tensor_tensor(
                b, PB[bank][:, :], B2R[:, ejw * 512:(ejw + 1) * 512], ALU.add),
                waits=[("s_f", NB * c + idx + 1), ("s_bcg", 16)],
                inc=("s_epF" if idx < NFA else "s_epA"))
        def ep_T(idx, c=c):
            if idx < NFA:
                bw = [("s_epF", NFA * c + idx + 1)]
            else:
                bw = [("s_epA", (NB - NFA) * c + (idx - NFA) + 1)]
            if idx >= 3:
                bw.append(("s_epC", NB * c + idx - 2))
            b = buf(idx)
            if last_chunk:
                emit("scalar", lambda e, b=b: e.activation(
                    b, b, ACT.Silu),
                    waits=bw, inc="s_epB")
            else:
                emit("scalar", lambda e, b=b, idx=idx: e.activation(
                    TSCR[:, idx % 3, :], b, ACT.Tanh, scale=0.5),
                    waits=bw, inc="s_epB")
        def ep_CD(idx, c=c, last_chunk=last_chunk):
            ts, ejw = seq[idx] // JW, seq[idx] % JW
            b = buf(idx)
            if last_chunk:
                emit("vector", lambda e, b=b, ts=ts, ejw=ejw: e.tensor_tensor(
                    b, b, SC[:, ts, ejw * 512:(ejw + 1) * 512], ALU.mult),
                    waits=[("s_epB", NB * c + idx + 1)], inc="s_epC")
                emit("vector", lambda e: e.nop(), inc="s_epC2")
                emit("vector", lambda e, b=b, ts=ts, ejw=ejw: e.tensor_tensor(
                    XBUF[:, ts, ejw * 512:(ejw + 1) * 512], b,
                    XBUF[:, ts, ejw * 512:(ejw + 1) * 512], ALU.add),
                    waits=[("s_epC", NB * c + idx + 1), ("s_xc", 16 * (c + 1))], inc="s_epD")
                return
            emit("vector", lambda e, b=b, idx=idx: e.scalar_tensor_tensor(
                b, TSCR[:, idx % 3, :], 1.0, b,
                op0=ALU.add, op1=ALU.mult),
                waits=[("s_epB", NB * c + idx + 1)], inc="s_epC")
            emit("vector", lambda e, b=b, ts=ts, ejw=ejw: e.tensor_tensor(
                b, b, SC[:, ts, ejw * 512:(ejw + 1) * 512], ALU.mult),
                waits=[("s_epC", NB * c + idx + 1)], inc="s_epC2")
            emit("vector", lambda e, b=b, ts=ts, ejw=ejw: e.scalar_tensor_tensor(
                XBUF[:, ts, ejw * 512:(ejw + 1) * 512], b, 0.5,
                XBUF[:, ts, ejw * 512:(ejw + 1) * 512],
                op0=ALU.mult, op1=ALU.add),
                waits=[("s_epC2", NB * c + idx + 1), ("s_xc", 16 * (c + 1))], inc="s_epD")
        def ep_out(ts, c=c):
            need = max(seq.index(ts * JW + e) for e in range(JW)) + 1
            emit("sync", lambda e, c=c, ts=ts: e.dma_start(
                out=o_h.ap()[c * CH + ts * 128:c * CH + (ts + 1) * 128, :],
                in_=XBUF[:, ts, :]),
                waits=[("s_epD", NB * c + need)], inc="s_out", amt=16)

        NHEAD = min(4, NB)
        ep_A(0)
        if NB > 1:
            ep_A(1)
        for idx in range(NHEAD):
            ep_T(idx)
            ep_CD(idx)
            if idx + 2 < NB:
                ep_A(idx + 2)
        for ts in range(NHEAD // JW):
            ep_out(ts)

        def make_tail(c=c, NHEAD=NHEAD):
            def tail():
                for j in range(min(NB, NHEAD + 2), NB):
                    ep_A(j, c)
                for idx in range(NHEAD, NB):
                    ep_T(idx, c)
                    ep_CD(idx, c)
                for ts in range(NHEAD // JW, SUB):
                    ep_out(ts, c)
                if c + 1 < NCH:
                    emit("sync", lambda e, c=c: e.dma_start(
                        out=XBUF[:, 0:SUB, :],
                        in_=x_h.ap()[(c + 1) * CH:(c + 2) * CH, :].rearrange(
                            "(s p) e -> p s e", p=128)),
                        waits=[("s_out", 16 * SUB * (c + 1))], inc="s_xc", amt=16)
            return tail
        if c + 1 < NCH:
            pending_tail[0] = make_tail()
        else:
            make_tail()()

    final_out = C["s_out"]
    final_dbg = C["s_dbg"]
    def final_wait(e):
        e.wait_ge(S["s_out"], final_out)
        if final_dbg:
            e.wait_ge(S["s_dbg"], final_dbg)

    floors = {eng: {} for eng in ("tensor", "vector", "scalar", "gpsimd", "sync")}

    def run_section(eng_name, eng_obj):
        fl = floors[eng_name]
        for ev in evs:
            if ev.eng != eng_name:
                continue
            for (sem, val) in ev.waits:
                if fl.get(sem.name, 0) >= val:
                    continue
                eng_obj.wait_ge(sem, val)
                fl[sem.name] = val
            ins = ev.fn(eng_obj)
            if ev.inc is not None:
                ins.then_inc(ev.inc[0], ev.inc[1])
        if eng_name == "sync":
            final_wait(eng_obj)

    with ctx:
        with nc.Block() as blk:
            blk.tensor(lambda e: run_section("tensor", e))
            blk.vector(lambda e: run_section("vector", e))
            blk.scalar(lambda e: run_section("scalar", e))
            blk.gpsimd(lambda e: run_section("gpsimd", e))
            blk.sync(lambda e: run_section("sync", e))
    return nc


def kernel(L0_attn_hidden_states, L0_attn_W_q, L0_ffn_w1, L0_ffn_b1, L0_ffn_w2, L0_ffn_b2):
    x = np.ascontiguousarray(np.asarray(L0_attn_hidden_states))
    wq = np.ascontiguousarray(np.asarray(L0_attn_W_q))
    w1 = np.ascontiguousarray(np.asarray(L0_ffn_w1))
    b1 = np.ascontiguousarray(np.asarray(L0_ffn_b1))
    w2 = np.ascontiguousarray(np.asarray(L0_ffn_w2))
    b2 = np.ascontiguousarray(np.asarray(L0_ffn_b2))
    assert x.shape == (8, 2048, 1024)

    nc = build_program(**FULL)
    in_maps = [
        {"x": x[b], "wq": wq, "w1": w1, "b1": b1, "w2": w2, "b2": b2}
        for b in range(N_CORES)
    ]
    res = run_bass_kernel_spmd(nc, in_maps, core_ids=list(range(N_CORES)))
    out = np.stack([np.asarray(res.results[b]["out"]) for b in range(N_CORES)], axis=0)
    return out.astype(np.float32)


# version 39
# speedup vs baseline: 1.0249x; 1.0249x over previous
import numpy as np
import concourse.bass as bass
import concourse.mybir as mybir
from concourse.bass_utils import run_bass_kernel_spmd
from contextlib import ExitStack

F32 = mybir.dt.float32
BF16 = mybir.dt.bfloat16
I32 = mybir.dt.int32
AX = mybir.AxisListType
ALU = mybir.AluOpType
ACT = mybir.ActivationFunctionType

FULL = dict(T=2048, D=1024, NF=4096)
N_CORES = 8


class Ev:
    __slots__ = ("eng", "fn", "waits", "inc")
    def __init__(self, eng, fn, waits, inc):
        self.eng, self.fn, self.waits, self.inc = eng, fn, waits, inc


def build_program(T, D, NF, debug=False):
    TC = T // 128
    DC = D // 128
    NC = NF // 128
    JW = D // 512
    AG = DC // 4
    HG = DC // 4
    NQ = TC // 4
    CH = 512 if T % 512 == 0 else T
    NCH = T // CH
    SUB = CH // 128
    NB = JW * SUB
    R1 = 3 if NB >= 3 else NB
    NSLOT = 8
    NPAIR = 5
    assert NB <= 8 and DC <= 8 and TC % 4 == 0 and NC % 2 == 0 and NC <= 32
    EPS = 1e-5

    nc = bass.Bass("TRN2", target_bir_lowering=False)

    x_h = nc.dram_tensor("x", [T, D], F32, kind="ExternalInput")
    wq_h = nc.dram_tensor("wq", [D, D], F32, kind="ExternalInput")
    w1_h = nc.dram_tensor("w1", [D, NF], F32, kind="ExternalInput")
    b1_h = nc.dram_tensor("b1", [NF], F32, kind="ExternalInput")
    w2_h = nc.dram_tensor("w2", [NF, D], F32, kind="ExternalInput")
    b2_h = nc.dram_tensor("b2", [D], F32, kind="ExternalInput")
    o_h = nc.dram_tensor("out", [T, D], F32, kind="ExternalOutput")
    w2scr = nc.dram_tensor("w2scr", [NF, D], BF16, kind="Internal")
    if debug:
        dbg_xt = nc.dram_tensor("dbg_xt", [128, TC, D], BF16, kind="ExternalOutput")
        dbg_xf = nc.dram_tensor("dbg_xf", [128, DC, T], BF16, kind="ExternalOutput")
        dbg_gs = nc.dram_tensor("dbg_gs", [128, DC, D], BF16, kind="ExternalOutput")
        dbg_mt = nc.dram_tensor("dbg_mt", [128, DC, D], BF16, kind="ExternalOutput")
        dbg_sc = nc.dram_tensor("dbg_sc", [128, SUB, D], F32, kind="ExternalOutput")
        dbg_h1 = nc.dram_tensor("dbg_h1", [128, NC, CH], BF16, kind="ExternalOutput")

    ARENA = 212700
    arena_ctx = ExitStack()
    arena = arena_ctx.enter_context(nc.sbuf_tensor("arena", [128, ARENA], mybir.dt.uint8))
    arena_base = nc.lookup_mloc(arena).addr

    off = [0]
    def sb(name, shape, dtype):
        import functools, operator
        per = functools.reduce(operator.mul, shape[1:]) * mybir.dt.size(dtype)
        h = nc.alloc_sbuf_tensor_at(name, shape, dtype, offset=arena_base + off[0])
        off[0] += (per + 31) // 32 * 32
        return h

    W1 = sb("W1s", [128, DC, NF], BF16)
    XF = sb("XFs", [128, DC, T], BF16)
    MT = sb("MTs", [128, DC, D], BF16)
    XBUF = sb("XBUFs", [128, max(SUB, 4), D], F32)
    B1 = sb("B1s", [128, NC], F32)
    B2R = sb("B2Rs", [128, D], BF16)
    STAT = sb("STATs", [128, TC * 18], F32)
    DUM = sb("DUMs", [128, 5], F32)
    tscr_off = off[0]
    TSCR = sb("TSCRs", [128, 3, 512], F32)
    FSCR = sb("FSCRs", [128, 3, 512], F32)
    save_off = off[0]
    off[0] = tscr_off
    W2TS = nc.alloc_sbuf_tensor_at("W2TSs", [128, 2, 1024], BF16,
                                   offset=arena_base + tscr_off)
    B1STG = sb("B1STGs", [128, 128], F32)
    IOT = sb("IOTs", [128, 256], I32)
    IDTF = sb("IDTFs", [128, 32], F32)
    IDT = sb("IDTs", [128, 128], BF16)
    off[0] = save_off
    ab_base = off[0]
    XT = sb("XTs", [128, TC, D], BF16)
    WQR = sb("WQRs", [128, 4, D], BF16)
    gs_off = off[0]
    GS = sb("GSs", [128, DC, D], BF16)
    XGS = nc.alloc_sbuf_tensor_at("XGSs", [128, 2, D], F32, offset=arena_base + gs_off)
    off[0] = ab_base
    H1 = sb("H1s", [128, NC, CH], BF16)
    W2S = sb("W2Ss", [128, NSLOT, D], BF16)
    SC = sb("SCs", [128, SUB, D], F32)
    end_b = off[0]
    reg_a_end = ab_base + (XT.shape[1] * XT.shape[2] * 2 + 31) // 32 * 32 \
        + (WQR.shape[1] * WQR.shape[2] * 2 + 31) // 32 * 32 \
        + (GS.shape[1] * GS.shape[2] * 2 + 31) // 32 * 32
    off[0] = reg_a_end
    XLDT = sb("XLDTs", [128, 2, D], F32)
    off[0] = reg_a_end
    WQR2 = sb("WQR2s", [128, 4, D], BF16)
    off[0] = max(end_b, reg_a_end + 2 * D * 4)
    assert off[0] <= ARENA, f"sbuf overflow {off[0]}"

    ctx = ExitStack()
    PB = [ctx.enter_context(nc.psum_tensor(f"pb{i}", [128, 512], F32)) for i in range(8)]

    sem_names = [
        "s_x0", "s_x1", "s_x2", "s_x3", "s_x4", "s_x5", "s_x6", "s_x7", "s_x8", "s_x9", "s_xc", "s_bc", "s_bcg", "s_w1",
        "s_wq0", "s_wq1", "s_wq2", "s_wq3", "s_wq4", "s_wq5", "s_wq6", "s_wq7", "s_wqc", "s_w2s",
        "s_w1b0", "s_w1b1", "s_w1b2", "s_w1b3", "s_w1b4", "s_w1b5", "s_w1b6", "s_w1b7",
        "s_w2c", "s_iotf", "s_b1t", "s_b1ev",
        "s_v1", "s_vc", "s_g1", "s_gxa", "s_gxd",
        "s_a1", "s_a2", "s_xn",
        "s_tq", "s_xfA", "s_xfB",
        "s_g", "s_gev", "s_m", "s_mev",
        "s_s", "s_sev", "s_h1", "s_h1ev", "s_f",
        "s_epA", "s_epB", "s_epC", "s_epC2", "s_epD", "s_epF",
        "s_out", "s_dm", "s_dbg",
    ] + [f"s_w2rp{i}" for i in range(NPAIR)]
    S = {n: ctx.enter_context(nc.semaphore(n)) for n in sem_names}
    C = {n: 0 for n in sem_names}

    evs = []
    def emit(eng, fn, waits=(), inc=None, amt=1):
        w = [(S[s], v) for (s, v) in waits if v > 0]
        if inc is not None:
            C[inc] += amt
            inc_t = (S[inc], amt)
        else:
            inc_t = None
        evs.append(Ev(eng, fn, w, inc_t))


    emit("gpsimd", lambda e: e.memset(DUM[:, :], 0.0), inc="s_dm")
    emit("gpsimd", lambda e: e.memset(DUM[:, 4:5], EPS), waits=[("s_dm", 1)], inc="s_dm")
    emit("gpsimd", lambda e: e.iota(IOT[:, 0:128], [[1, 128]], channel_multiplier=0),
         inc="s_g1")
    emit("gpsimd", lambda e: e.iota(IOT[:, 128:256], [[0, 128]], channel_multiplier=1),
         inc="s_g1")
    emit("vector", lambda e: e.tensor_tensor(IDT[:, :], IOT[:, 0:128], IOT[:, 128:256],
                                             ALU.is_equal),
         waits=[("s_g1", 2)], inc="s_iotf")
    emit("vector", lambda e: e.tensor_copy(IDTF[:, 0:32], IDT[:, 0:32]),
         waits=[("s_iotf", 1)], inc="s_iotf")

    NWQ = HG * DC
    NXS = 8
    xldt_tiles = [i for i in range(TC) if i % NXS in (4, 5)]
    assert D == 1024 or TC <= 6 or True
    XLDT_LAST = (max(xldt_tiles) + 1) if xldt_tiles else 0

    def wq_slot_ap(L):
        s = L % 8
        return WQR[:, s, :] if s < 4 else WQR2[:, s - 4, :]

    def wq_load(L):
        r = L % DC
        wts = []
        if 4 <= L < 8 and XLDT_LAST:
            wts = [("s_gxa", 2 * XLDT_LAST), ("s_v1", 2 * XLDT_LAST)]
        emit("gpsimd", lambda e, L=L, r=r: e.dma_start(
            out=wq_slot_ap(L), in_=wq_h.ap()[r * 128:(r + 1) * 128, :]),
            waits=wts, inc=f"s_wq{L % 8}", amt=16)

    emit("gpsimd", lambda e: e.dma_start(
        out=B2R[:, :], in_=bass.AP(b2_h, 0, [[0, 128], [1, D]])), inc="s_bcg", amt=16)
    for L in range(min(8, DC)):
        wq_load(L)
    NBLK = NF // 512
    for nblk in range(NBLK):
        for hc in range(DC):
            emit("gpsimd", lambda e, hc=hc, nblk=nblk: e.dma_start(
                out=W1[:, hc, nblk * 512:(nblk + 1) * 512],
                in_=w1_h.ap()[hc * 128:(hc + 1) * 128, nblk * 512:(nblk + 1) * 512]),
                waits=[("s_v1", 2 * TC)], inc=f"s_w1b{nblk}", amt=16)
    emit("gpsimd", lambda e: e.dma_start(out=w2scr.ap(), in_=w2_h.ap()),
         waits=[("s_v1", 2 * TC)], inc="s_w2s", amt=16)


    emit("sync", lambda e: e.dma_start(
        out=B1STG[0:NC, 0:128], in_=b1_h.ap().rearrange("(a q) -> a q", q=128)),
        inc="s_bc", amt=16)
    emit("tensor", lambda e: e.transpose(PB[0][:, 0:NC], B1STG[0:NC, 0:128],
                                         IDTF[0:NC, 0:NC]),
         waits=[("s_bc", 16), ("s_iotf", 2)], inc="s_b1t")
    emit("vector", lambda e: e.tensor_copy(B1[:, :], PB[0][:, 0:NC]),
         waits=[("s_b1t", 1)], inc="s_b1ev")

    emit("scalar", lambda e: e.activation(DUM[:, 1:2], DUM[:, 0:1], ACT.Sqrt),
         waits=[("s_dm", 1)])


    def _xsl(i, a, b):
        s = i % NXS
        if s < 4:
            return XBUF[:, s, a:b]
        if s < 6:
            return XLDT[:, s - 4, a:b]
        if s == 6:
            return bass.AP(TSCR, 512 + a, [[3 * 512, 128], [1, b - a]])
        if s == 7:
            return bass.AP(FSCR, a, [[3 * 512, 128], [1, b - a]])
        return XGS[:, s - 8, a:b]

    def xslot(i):
        return _xsl(i, 0, D)

    def p1_load(i):
        slot = i % NXS
        w = []
        if i >= NXS:
            w = [("s_gxa", 2 * (i - NXS + 1)), ("s_v1", 2 * (i - NXS + 1))]
        emit("sync", lambda e, i=i: e.dma_start(
            out=xslot(i), in_=x_h.ap()[i * 128:(i + 1) * 128, :]),
            waits=w, inc=f"s_x{slot}", amt=16)

    def p1_head_lo(i):
        slot = i % NXS
        c0 = i * 18
        sx = f"s_x{slot}"
        D2 = D // 2
        emit("vector", lambda e, i=i, c0=c0, D2=D2: e.bn_stats(
            STAT[:, c0:c0 + 6], _xsl(i, 0, D2)),
            waits=[(sx, 16 * (i // NXS + 1))], inc="s_v1")

    def p1_head_hi(i):
        c0 = i * 18
        D2 = D // 2
        emit("vector", lambda e, i=i, c0=c0, D2=D2: e.bn_stats(
            STAT[:, c0 + 6:c0 + 12], _xsl(i, D2, D)), inc="s_v1")

    vc_at = {}

    def p1_stage1(i):
        c0 = i * 18
        emit("vector", lambda e, c0=c0: e.bn_aggr(
            STAT[:, c0 + 12:c0 + 14], STAT[:, c0:c0 + 12]),
            waits=[("s_v1", 2 * (i + 1))], inc="s_vc")
        vc_at[("aggr", i)] = C["s_vc"]

    def p1_stage2(i):
        c0 = i * 18
        emit("scalar", lambda e, c0=c0: e.activation(
            STAT[:, c0 + 14:c0 + 15], STAT[:, c0 + 13:c0 + 14], ACT.Sqrt,
            bias=DUM[:, 4:5], scale=1.0),
            waits=[("s_vc", vc_at[("aggr", i)])] + ([("s_dm", 2)] if i == 0 else []),
            inc="s_a2")
        emit("vector", lambda e, c0=c0: e.reciprocal(
            STAT[:, c0 + 15:c0 + 16], STAT[:, c0 + 14:c0 + 15]),
            waits=[("s_a2", i + 1)], inc="s_vc")
        vc_at[("r", i)] = C["s_vc"]
        emit("vector", lambda e, c0=c0: e.tensor_scalar(
            STAT[:, c0 + 16:c0 + 17], STAT[:, c0 + 12:c0 + 13], STAT[:, c0 + 15:c0 + 16],
            -1.0, op0=ALU.mult, op1=ALU.mult),
            waits=[("s_vc", vc_at[("r", i)])], inc="s_vc")
        vc_at[("bias2", i)] = C["s_vc"]

    def p1_stage3(i):
        c0 = i * 18
        D2 = D // 2
        emit("scalar", lambda e, i=i, c0=c0, D2=D2: e.activation(
            XT[:, i, 0:D2], _xsl(i, 0, D2), ACT.Identity,
            bias=STAT[:, c0 + 16:c0 + 17], scale=STAT[:, c0 + 15:c0 + 16]),
            waits=[("s_vc", vc_at[("bias2", i)])], inc="s_gxa")
        emit("scalar", lambda e, i=i, c0=c0, D2=D2: e.activation(
            XT[:, i, D2:D], _xsl(i, D2, D), ACT.Identity,
            bias=STAT[:, c0 + 16:c0 + 17], scale=STAT[:, c0 + 15:c0 + 16]),
            inc="s_gxa")

    NPH = (DC + 3) // 4
    xfA_cum, xfB_cum = [], []

    def g_round(w, b, ag, r):
        for ai in range(4):
            wts = []
            if r == 0 and ai == 0:
                if w == 0:
                    wts.append(("s_b1ev", 1))
                else:
                    wts.append(("s_gev", 4 * w))
            if w == 0 and ai == 0:
                wts.append(("s_gxa", 2 * (r + 1)))
            emit("tensor", lambda e, r=r, a=ai, b=b, ag=ag: e.matmul(
                PB[a][:, :], XT[:, r, (ag * 4 + a) * 128:(ag * 4 + a + 1) * 128],
                XT[:, r, b * 512:(b + 1) * 512],
                start=(r == 0), stop=(r == TC - 1)),
                waits=wts, inc=("s_g" if r == TC - 1 else None))

    def g_evacs(w, b, ag):
        for ai in range(4):
            emit("scalar", lambda e, a=ai, b=b, ag=ag: e.copy(
                GS[:, ag * 4 + a, b * 512:(b + 1) * 512], PB[a][:, :]),
                waits=[("s_g", 4 * w + ai + 1)], inc="s_gev")

    def tp_quad(q):
        for P in range(NPH):
            for j in range(4):
                for hcl in range(4):
                    hc = P * 4 + hcl
                    if hc >= DC:
                        continue
                    bank = 4 + hcl
                    wts = []
                    if j == 0 and hcl == 0:
                        wts += [("s_gxa", 2 * (4 * q + 4)), ("s_iotf", 2)]
                        if P == 0 and q > 0:
                            wts += [("s_xfA", xfA_cum[q - 1]), ("s_xfB", xfB_cum[q - 1])]
                    if j == 0 and P > 0:
                        if hcl < 2:
                            wts.append(("s_xfA", C["s_xfA"] - 1 + hcl))
                        else:
                            wts.append(("s_xfB", C["s_xfB"] - 1 + (hcl - 2)))
                    emit("tensor", lambda e, q=q, j=j, hc=hc, bank=bank: e.matmul(
                        PB[bank][:, j * 128:(j + 1) * 128],
                        XT[:, 4 * q + j, hc * 128:(hc + 1) * 128], IDT[:, :],
                        start=True, stop=True),
                        waits=wts, inc=("s_tq" if j == 3 else None))
            for hcl in range(4):
                hc = P * 4 + hcl
                if hc >= DC:
                    continue
                eng = "scalar" if hcl < 2 else "vector"
                sem = "s_xfA" if hcl < 2 else "s_xfB"
                emit(eng, lambda e, q=q, hc=hc, hcl=hcl:
                     (e.copy if hcl < 2 else e.tensor_copy)(
                         XF[:, hc, q * 512:(q + 1) * 512], PB[4 + hcl][:, :]),
                     waits=[("s_tq", C["s_tq"] - (3 - hcl))], inc=sem)
        xfA_cum.append(C["s_xfA"])
        xfB_cum.append(C["s_xfB"])

    for i in range(TC + 2):
        if i < TC:
            p1_load(i)
        if 1 <= i <= TC:
            p1_stage1(i - 1)
        if i < TC:
            p1_head_lo(i)
        if 1 <= i <= TC:
            p1_stage2(i - 1)
        if i < TC:
            p1_head_hi(i)
        if 2 <= i <= TC + 1:
            p1_stage3(i - 2)
            g_round(0, 0, 0, i - 2)
            if (i - 2) % 4 == 3:
                tp_quad((i - 2) // 4)
    g_evacs(0, 0, 0)
    XF_A_ALL = C["s_xfA"]
    XF_B_ALL = C["s_xfB"]

    w = 1
    for b in range(JW):
        for ag in range(AG):
            if b == 0 and ag == 0:
                continue
            if b == 1 and ag == 0 and JW == 2 and AG == 2:
                for j in range(4):
                    for i2 in range(4):
                        wts = []
                        if j == 0 and i2 == 0:
                            wts = [("s_gev", 8), ("s_iotf", 2)]
                        emit("tensor", lambda e, j=j, i2=i2: e.matmul(
                            PB[j][:, i2 * 128:(i2 + 1) * 128],
                            GS[:, 4 + i2, j * 128:(j + 1) * 128], IDT[:, :],
                            start=True, stop=True),
                            waits=wts, inc=("s_g" if i2 == 3 else None))
                for j in range(4):
                    emit("scalar", lambda e, j=j: e.copy(
                        GS[:, j, 512:1024], PB[j][:, :]),
                        waits=[("s_g", 8 + j + 1)], inc="s_gev")
                w += 1
                continue
            for r in range(TC):
                g_round(w, b, ag, r)
            g_evacs(w, b, ag)
            w += 1
    G_EV_ALL = C["s_gev"]

    def emit_M():
        for hg in range(HG):
            for r in range(DC):
                L = hg * DC + r
                nmm = 4 * JW
                for mi, (hci, jw) in enumerate([(a2, b2) for a2 in range(4) for b2 in range(JW)]):
                    bank = hci * JW + jw
                    wts = []
                    if mi == 0:
                        wts.append((f"s_wq{r % 8}", 16))
                        wts.append(("s_gev", (JW - 1) * DC + r + 1))
                        if r == 0 and hg > 0:
                            wts.append(("s_mev", 4 * JW * hg))
                        if r == 0 and hg == 0:
                            wts.append(("s_gev", G_EV_ALL))
                            if FFN1_FIRST:
                                wts.append(("s_h1ev", NC))
                            else:
                                wts.append(("s_xfA", XF_A_ALL))
                                wts.append(("s_xfB", XF_B_ALL))
                    if r == DC - 1:
                        inc = "s_m"
                    elif mi == nmm - 1:
                        inc = "s_wqc"
                    else:
                        inc = None
                    emit("tensor", lambda e, hg=hg, r=r, hci=hci, jw=jw, L=L, bank=bank: e.matmul(
                        PB[bank][:, :], wq_slot_ap(r)[:, (hg * 4 + hci) * 128:(hg * 4 + hci + 1) * 128],
                        GS[:, r, jw * 512:(jw + 1) * 512],
                        start=(r == 0), stop=(r == DC - 1)),
                        waits=wts, inc=inc)
            for hci in range(4):
                for jw in range(JW):
                    idx = hci * JW + jw
                    emit("vector", lambda e, hg=hg, hci=hci, jw=jw, idx=idx: e.tensor_copy(
                        MT[:, hg * 4 + hci, jw * 512:(jw + 1) * 512], PB[idx][:, :]),
                        waits=[("s_m", 4 * JW * hg + idx + 1)], inc="s_mev")
        return C["s_m"], C["s_mev"]

    M_ALL = 4 * JW * HG
    M_EV_ALL = 4 * JW * HG
    FFN1_FIRST = NC * CH * 2 <= TC * D * 2
    if not FFN1_FIRST:
        m_all, m_ev_all = emit_M()
        assert m_all == M_ALL and m_ev_all == M_EV_ALL

    emit("sync", lambda e: e.dma_start(
        out=XBUF[:, 0:SUB, :], in_=x_h.ap()[0:CH, :].rearrange("(s p) e -> p s e", p=128)),
        waits=[("s_gxa", 2 * TC), ("s_v1", 2 * TC)], inc="s_xc", amt=16)

    emit("scalar", lambda e: e.activation(DUM[:, 2:3], DUM[:, 0:1], ACT.Gelu))

    if debug:
        dw = [("s_mev", M_EV_ALL), ("s_gev", G_EV_ALL), ("s_gxa", 2 * TC),
              ("s_xfA", XF_A_ALL), ("s_xfB", XF_B_ALL)]
        emit("sync", lambda e: e.dma_start(out=dbg_xt.ap(), in_=XT[:, :, :]),
             waits=dw, inc="s_dbg", amt=16)
        emit("sync", lambda e: e.dma_start(out=dbg_xf.ap(), in_=XF[:, :, :]), inc="s_dbg", amt=16)
        emit("sync", lambda e: e.dma_start(out=dbg_gs.ap(), in_=GS[:, :, :]), inc="s_dbg", amt=16)
        emit("sync", lambda e: e.dma_start(out=dbg_mt.ap(), in_=MT[:, :, :]), inc="s_dbg", amt=16)

    w2rp_cnt = [0] * NPAIR
    w2_avail = {}
    pending_tail = [None]
    for c in range(NCH):
        npass = 2 if (c == NCH - 1 and JW == 2) else 1

        XRING = (c == NCH - 1 and JW == 2 and NC > NSLOT)
        NSX = NSLOT + 2 if XRING else NSLOT

        def w2pair_out(s):
            return W2S[:, s:s + 2, :] if s < NSLOT else W2TS[:, 0:2, :]

        def ref_dma(rows0, s, wts, p):
            w2rp_cnt[p] += 16
            emit("sync", lambda e, rows0=rows0, s=s: e.dma_start(
                out=w2pair_out(s),
                in_=w2scr.ap()[rows0 * 128:(rows0 + 2) * 128, :].rearrange(
                    "(k p) e -> p k e", p=128)),
                waits=wts, inc=f"s_w2rp{p}", amt=16)

        def emit_refills(ps, c=c, XRING=XRING, NSX=NSX):
            if ps == 0:
                for pk in range(NC // 2):
                    nk0 = 2 * pk
                    s = nk0 % NSX
                    if XRING:
                        cp = NCH - 2
                        if nk0 >= NSX:
                            wts = [("s_w2c", (NCH - 1) * (NC - 1) + (nk0 - NSX + 1) + 1)]
                        elif s >= NSLOT:
                            wts = [("s_epC", NB * (cp + 1))]
                        else:
                            nko = NC - NSLOT + s + 1
                            if nko < NC - 1:
                                wts = [("s_w2c", cp * (NC - 1) + nko + 1)]
                            elif nko == NC - 1:
                                wts = [("s_w2c", cp * (NC - 1) + nko),
                                       ("s_f", NB * (cp + 1))]
                            else:
                                wts = [("s_f", NB * (cp + 1))]
                    else:
                        Rg = NC * c + nk0
                        if Rg < NSLOT:
                            wts = [("s_m", M_ALL), ("s_w2s", 16)]
                        else:
                            Rp = Rg - NSLOT + 1
                            c_p, nk_p = Rp // NC, Rp % NC
                            if nk_p < NC - 1:
                                wts = [("s_w2c", c_p * (NC - 1) + nk_p + 1)]
                            else:
                                wts = [("s_f", NB * c_p + (SUB if (c_p == NCH - 1 and JW == 2) else NB))]
                    p = s // 2
                    w2_avail[(c, 0, nk0)] = w2_avail[(c, 0, nk0 + 1)] = (p, w2rp_cnt[p] + 16)
                    ref_dma(nk0, s, wts, p)
            else:
                base = NCH * (NC - 1)
                for r in range(NC - NSX - 2, -1, -2):
                    wts = [("s_w2c", base + NC - (r + NSX))]
                    s = r % NSX
                    p = s // 2
                    w2_avail[(c, 1, r)] = w2_avail[(c, 1, r + 1)] = (p, w2rp_cnt[p] + 16)
                    ref_dma(r, s, wts, p)

        emit_refills(0)

        bank0 = 4 if (c == 0 and FFN1_FIRST) else 0
        for na in range(NC):
            bank = bank0 + na % R1
            wts = []
            if na == 0:
                wts += [("s_xfA", xfA_cum[min(c, NQ - 1)]),
                        ("s_xfB", xfB_cum[min(c, NQ - 1)])]
            if c == 0:
                if na == 0:
                    wts += [("s_xfA", XF_A_ALL), ("s_xfB", XF_B_ALL)]
                    if not FFN1_FIRST:
                        wts += [("s_mev", M_EV_ALL)]
                elif na >= R1:
                    wts = [("s_h1ev", NC * c + (na - R1) + 1)]
            else:
                if na == 0:
                    wts += [(f"s_w1b{b}", 16 * DC) for b in range(NF // 512)]
                if na < R1:
                    wts += [("s_epF", min(3, NB) * (c - 1) + (na % R1) + 1)]
                else:
                    wts = [("s_h1ev", NC * c + (na - R1) + 1)]
            for hc in range(DC):
                hw = list(wts) if hc == 0 else []
                if c == 0 and hc == 0:
                    hw.append((f"s_w1b{na // 4}", 16 * DC))
                emit("tensor", lambda e, na=na, hc=hc, c=c, bank=bank: e.matmul(
                    PB[bank][:, 0:CH], W1[:, hc, na * 128:(na + 1) * 128],
                    XF[:, hc, c * CH:(c + 1) * CH],
                    start=(hc == 0), stop=(hc == DC - 1)),
                    waits=hw, inc=("s_h1" if hc == DC - 1 else None))
        for na in range(NC):
            emit("scalar", lambda e, na=na, bank=bank0 + na % R1: e.activation(
                H1[:, na, :], PB[bank][:, 0:CH], ACT.Gelu, bias=B1[:, na:na + 1], scale=1.0),
                waits=[("s_h1", NC * c + na + 1), ("s_b1ev", 1)], inc="s_h1ev")

        if c == 0 and FFN1_FIRST:
            m_all, m_ev_all = emit_M()
            assert m_all == M_ALL and m_ev_all == M_EV_ALL

        if pending_tail[0] is not None:
            pending_tail[0]()
            pending_tail[0] = None

        if npass == 2:
            emit_refills(1)

        for g in range(NB):
            w2, ts = g // SUB, g % SUB
            wts = []
            if c == 0:
                if g == 0:
                    wts.append(("s_mev", M_EV_ALL))
                if not FFN1_FIRST and g < R1:
                    last_na = max(n for n in range(NC) if n % R1 == g)
                    wts.append(("s_h1ev", NC * c + last_na + 1))
            elif g < R1:
                last_na = max(n for n in range(NC) if n % R1 == g)
                wts.append(("s_h1ev", NC * c + last_na + 1))
            else:
                wts.append(("s_epD", NB * (c - 1) + g + 1))
            for hc in range(DC):
                emit("tensor", lambda e, g=g, w2=w2, ts=ts, hc=hc, c=c: e.matmul(
                    PB[g][:, :], XF[:, hc, c * CH + ts * 128:c * CH + (ts + 1) * 128],
                    MT[:, hc, w2 * 512:(w2 + 1) * 512],
                    start=(hc == 0), stop=(hc == DC - 1)),
                    waits=(wts if hc == 0 else []), inc=("s_s" if hc == DC - 1 else None))
        for g in range(NB):
            w2, ts = g // SUB, g % SUB
            emit("vector", lambda e, g=g, w2=w2, ts=ts: e.tensor_copy(
                SC[:, ts, w2 * 512:(w2 + 1) * 512], PB[g][:, :]),
                waits=[("s_s", NB * c + g + 1), ("s_epC", NB * c)], inc="s_sev")

        split2 = (c == NCH - 1 and JW == 2)
        if not split2:
            for nk in range(NC):
                wts = []
                if nk == 0:
                    wts = [("s_h1ev", NC * (c + 1)), ("s_sev", NB * (c + 1))]
                else:
                    wts = [("s_h1ev", NC * c + nk + 1)]
                pi, cnt = w2_avail[(c, 0, nk)]
                wts.append((f"s_w2rp{pi}", cnt))
                pairs = [(ts, ejw) for ts in range(SUB) for ejw in range(JW)]
                for mi, (ts, ejw) in enumerate(pairs):
                    bank = ts * JW + ejw
                    if nk == NC - 1:
                        inc = "s_f"
                    elif mi == len(pairs) - 1:
                        inc = "s_w2c"
                    else:
                        inc = None
                    emit("tensor", lambda e, nk=nk, ts=ts, ejw=ejw, bank=bank: e.matmul(
                        PB[bank][:, :], H1[:, nk, ts * 128:(ts + 1) * 128],
                        W2S[:, nk % NSLOT, ejw * 512:(ejw + 1) * 512],
                        start=(nk == 0), stop=(nk == NC - 1)),
                        waits=(wts if mi == 0 else []), inc=inc)
        else:
            for ep in range(2):
                order = list(range(NC)) if ep == 0 else list(range(NC - 1, -1, -1))
                for vi, nk in enumerate(order):
                    wts = []
                    if ep == 0 and nk == 0:
                        wts = [("s_h1ev", NC * (c + 1)), ("s_sev", NB * (c + 1))]
                    elif ep == 0:
                        wts = [("s_h1ev", NC * c + nk + 1)]
                    if (c, ep, nk) in w2_avail:
                        pi, cnt = w2_avail[(c, ep, nk)]
                        wts.append((f"s_w2rp{pi}", cnt))
                    first, last = (vi == 0), (vi == NC - 1)
                    for ts in range(SUB):
                        bank = 2 * ts + ep
                        if last:
                            inc = "s_f"
                        elif ts == SUB - 1:
                            inc = "s_w2c"
                        else:
                            inc = None
                        sl = nk % NSX
                        rhs = (lambda sl=sl, ep=ep: W2S[:, sl, ep * 512:(ep + 1) * 512]
                               if sl < NSLOT else
                               W2TS[:, sl - NSLOT, ep * 512:(ep + 1) * 512])
                        emit("tensor", lambda e, nk=nk, ts=ts, ep=ep, bank=bank,
                             first=first, last=last, rhs=rhs: e.matmul(
                            PB[bank][:, :], H1[:, nk, ts * 128:(ts + 1) * 128],
                            rhs(),
                            start=first, stop=last),
                            waits=(wts if ts == 0 else []), inc=inc)

        if debug and c == 0:
            emit("sync", lambda e: e.dma_start(out=dbg_sc.ap(), in_=SC[:, :, :]),
                 waits=[("s_f", NB)], inc="s_dbg", amt=16)
            emit("sync", lambda e: e.dma_start(out=dbg_h1.ap(), in_=H1[:, :, :]), inc="s_dbg", amt=16)

        last_chunk = (c == NCH - 1)
        if last_chunk:
            emit("scalar", lambda e: e.activation(DUM[:, 3:4], DUM[:, 0:1], ACT.Silu))
        NFA = min(3, NB)
        if last_chunk and JW == 2:
            seq = [2 * t for t in range(SUB)] + [2 * t + 1 for t in range(SUB)]
        else:
            seq = list(range(NB))
        def buf(idx, _seq=seq, _NFA=NFA):
            return FSCR[:, idx, :] if idx < _NFA else PB[_seq[idx]][:, :]
        def ep_A(idx, c=c):
            bank = seq[idx]
            ejw = bank % JW
            b = buf(idx)
            emit("vector", lambda e, b=b, bank=bank, ejw=ejw: e.tensor_tensor(
                b, PB[bank][:, :], B2R[:, ejw * 512:(ejw + 1) * 512], ALU.add),
                waits=[("s_f", NB * c + idx + 1), ("s_bcg", 16)],
                inc=("s_epF" if idx < NFA else "s_epA"))
        def ep_T(idx, c=c):
            if idx < NFA:
                bw = [("s_epF", NFA * c + idx + 1)]
            else:
                bw = [("s_epA", (NB - NFA) * c + (idx - NFA) + 1)]
            if idx >= 3:
                bw.append(("s_epC", NB * c + idx - 2))
            b = buf(idx)
            if last_chunk:
                emit("scalar", lambda e, b=b: e.activation(
                    b, b, ACT.Silu),
                    waits=bw, inc="s_epB")
            else:
                emit("scalar", lambda e, b=b, idx=idx: e.activation(
                    TSCR[:, idx % 3, :], b, ACT.Tanh, scale=0.5),
                    waits=bw, inc="s_epB")
        def ep_CD(idx, c=c, last_chunk=last_chunk):
            ts, ejw = seq[idx] // JW, seq[idx] % JW
            b = buf(idx)
            if last_chunk:
                emit("vector", lambda e, b=b, ts=ts, ejw=ejw: e.tensor_tensor(
                    b, b, SC[:, ts, ejw * 512:(ejw + 1) * 512], ALU.mult),
                    waits=[("s_epB", NB * c + idx + 1)], inc="s_epC")
                emit("vector", lambda e: e.nop(), inc="s_epC2")
                emit("vector", lambda e, b=b, ts=ts, ejw=ejw: e.tensor_tensor(
                    XBUF[:, ts, ejw * 512:(ejw + 1) * 512], b,
                    XBUF[:, ts, ejw * 512:(ejw + 1) * 512], ALU.add),
                    waits=[("s_epC", NB * c + idx + 1), ("s_xc", 16 * (c + 1))], inc="s_epD")
                return
            emit("vector", lambda e, b=b, idx=idx: e.scalar_tensor_tensor(
                b, TSCR[:, idx % 3, :], 1.0, b,
                op0=ALU.add, op1=ALU.mult),
                waits=[("s_epB", NB * c + idx + 1)], inc="s_epC")
            emit("vector", lambda e, b=b, ts=ts, ejw=ejw: e.tensor_tensor(
                b, b, SC[:, ts, ejw * 512:(ejw + 1) * 512], ALU.mult),
                waits=[("s_epC", NB * c + idx + 1)], inc="s_epC2")
            emit("vector", lambda e, b=b, ts=ts, ejw=ejw: e.scalar_tensor_tensor(
                XBUF[:, ts, ejw * 512:(ejw + 1) * 512], b, 0.5,
                XBUF[:, ts, ejw * 512:(ejw + 1) * 512],
                op0=ALU.mult, op1=ALU.add),
                waits=[("s_epC2", NB * c + idx + 1), ("s_xc", 16 * (c + 1))], inc="s_epD")
        def ep_out(ts, c=c):
            need = max(seq.index(ts * JW + e) for e in range(JW)) + 1
            emit("sync", lambda e, c=c, ts=ts: e.dma_start(
                out=o_h.ap()[c * CH + ts * 128:c * CH + (ts + 1) * 128, :],
                in_=XBUF[:, ts, :]),
                waits=[("s_epD", NB * c + need)], inc="s_out", amt=16)

        NHEAD = min(4, NB)
        ep_A(0)
        if NB > 1:
            ep_A(1)
        for idx in range(NHEAD):
            ep_T(idx)
            ep_CD(idx)
            if idx + 2 < NB:
                ep_A(idx + 2)
        for ts in range(NHEAD // JW):
            ep_out(ts)

        def make_tail(c=c, NHEAD=NHEAD):
            def tail():
                for j in range(min(NB, NHEAD + 2), NB):
                    ep_A(j, c)
                for idx in range(NHEAD, NB):
                    ep_T(idx, c)
                    ep_CD(idx, c)
                for ts in range(NHEAD // JW, SUB):
                    ep_out(ts, c)
                if c + 1 < NCH:
                    emit("sync", lambda e, c=c: e.dma_start(
                        out=XBUF[:, 0:SUB, :],
                        in_=x_h.ap()[(c + 1) * CH:(c + 2) * CH, :].rearrange(
                            "(s p) e -> p s e", p=128)),
                        waits=[("s_out", 16 * SUB * (c + 1))], inc="s_xc", amt=16)
            return tail
        if c + 1 < NCH:
            pending_tail[0] = make_tail()
        else:
            make_tail()()

    final_out = C["s_out"]
    final_dbg = C["s_dbg"]
    def final_wait(e):
        e.wait_ge(S["s_out"], final_out)
        if final_dbg:
            e.wait_ge(S["s_dbg"], final_dbg)

    floors = {eng: {} for eng in ("tensor", "vector", "scalar", "gpsimd", "sync")}

    def run_section(eng_name, eng_obj):
        fl = floors[eng_name]
        for ev in evs:
            if ev.eng != eng_name:
                continue
            for (sem, val) in ev.waits:
                if fl.get(sem.name, 0) >= val:
                    continue
                eng_obj.wait_ge(sem, val)
                fl[sem.name] = val
            ins = ev.fn(eng_obj)
            if ev.inc is not None:
                ins.then_inc(ev.inc[0], ev.inc[1])
        if eng_name == "sync":
            final_wait(eng_obj)

    with ctx:
        with nc.Block() as blk:
            blk.tensor(lambda e: run_section("tensor", e))
            blk.vector(lambda e: run_section("vector", e))
            blk.scalar(lambda e: run_section("scalar", e))
            blk.gpsimd(lambda e: run_section("gpsimd", e))
            blk.sync(lambda e: run_section("sync", e))
    return nc


def kernel(L0_attn_hidden_states, L0_attn_W_q, L0_ffn_w1, L0_ffn_b1, L0_ffn_w2, L0_ffn_b2):
    x = np.ascontiguousarray(np.asarray(L0_attn_hidden_states))
    wq = np.ascontiguousarray(np.asarray(L0_attn_W_q))
    w1 = np.ascontiguousarray(np.asarray(L0_ffn_w1))
    b1 = np.ascontiguousarray(np.asarray(L0_ffn_b1))
    w2 = np.ascontiguousarray(np.asarray(L0_ffn_w2))
    b2 = np.ascontiguousarray(np.asarray(L0_ffn_b2))
    assert x.shape == (8, 2048, 1024)

    nc = build_program(**FULL)
    in_maps = [
        {"x": x[b], "wq": wq, "w1": w1, "b1": b1, "w2": w2, "b2": b2}
        for b in range(N_CORES)
    ]
    res = run_bass_kernel_spmd(nc, in_maps, core_ids=list(range(N_CORES)))
    out = np.stack([np.asarray(res.results[b]["out"]) for b in range(N_CORES)], axis=0)
    return out.astype(np.float32)
